# version 27
# baseline (speedup 1.0000x reference)
import numpy as np
import contextlib
import concourse.bass as bass
import concourse.mybir as mybir
from concourse.bass_utils import run_bass_kernel_spmd

F32 = mybir.dt.float32
BF16 = mybir.dt.bfloat16
AF = mybir.ActivationFunctionType
OP = mybir.AluOpType
AXX = mybir.AxisListType.X

D = 4096
H = 8
DH = 128
QD = H * DH
NT = D // 128
NCORES = 8


class _Op:
    __slots__ = ("eng", "fn", "deps", "dma", "key", "need", "cum", "sem", "tag")

    def __init__(self, eng, fn, deps, dma, key, tag):
        self.eng, self.fn, self.dma, self.key, self.tag = eng, fn, dma, key, tag
        self.deps = [d for d in deps if d is not None]
        self.need = False
        self.cum = None
        self.sem = None


class Plan:

    def __init__(self):
        self.ops = []

    def add(self, eng, fn, deps=(), dma=False, key=None, tag=""):
        assert (key is not None) == dma, tag
        op = _Op(eng, fn, deps, dma, key, tag)
        self.ops.append(op)
        return op

    def finalize(self):
        for op in self.ops:
            if op.dma:
                op.need = True
            for d in op.deps:
                d.need = True
        cums = {}
        self.keys = []
        for op in self.ops:
            if op.dma:
                op.sem = "dma_" + op.key
                if op.key not in self.keys:
                    self.keys.append(op.key)
            else:
                op.sem = op.eng
            if op.need:
                cums[op.sem] = cums.get(op.sem, 0) + (16 if op.dma else 1)
                op.cum = cums[op.sem]
        self.per_engine = {}
        for op in self.ops:
            self.per_engine.setdefault(op.eng, []).append(op)

    def emit(self, ename, eng, sems):
        waited = {}
        for op in self.per_engine.get(ename, []):
            for d in op.deps:
                assert d.cum is not None, f"{op.tag} dep {d.tag} not inc'd"
                if waited.get(d.sem, 0) >= d.cum:
                    continue
                eng.wait_ge(sems[d.sem], d.cum)
                waited[d.sem] = d.cum
            instr = op.fn(eng)
            if isinstance(instr, list):
                instr = instr[-1]
            if op.need:
                assert instr is not None, f"op {op.tag} needs inc but returned None"
                instr.then_inc(sems[op.sem], 16 if op.dma else 1)


def build_nc(R):
    NS = R // 128
    nc = bass.Bass("TRN2", target_bir_lowering=False)

    x_d = nc.dram_tensor("x", [R, D], F32, kind="ExternalInput").ap()
    xt_d = nc.dram_tensor("xt", [R // 128, 128, D // 128, 128], F32,
                          kind="ExternalInput").ap()
    wq_d = nc.dram_tensor("wq", [NT, 128, QD], F32, kind="ExternalInput").ap()
    bq_d = nc.dram_tensor("bq", [QD], F32, kind="ExternalInput").ap()
    wo_d = nc.dram_tensor("wo", [H, 4, 128, D], F32, kind="ExternalInput").ap()
    m_d = nc.dram_tensor("m", [H, DH, DH], F32, kind="ExternalInput").ap()
    z_d = nc.dram_tensor("z", [H, DH], F32, kind="ExternalInput").ap()
    wf_d = nc.dram_tensor("wf", [NT, 128, D], F32, kind="ExternalInput").ap()
    sc_d = nc.dram_tensor("sc", [32, 32], F32, kind="ExternalInput").ap()
    y_d = nc.dram_tensor("y", [R, D], F32, kind="ExternalOutput").ap()
    scr1 = nc.dram_tensor("scr1", [NS, 128, NT, 128], F32, kind="Internal").ap()
    scr2 = nc.dram_tensor("scr2", [NS, 128, NT, 128], F32, kind="Internal").ap()

    P = Plan()
    ctx = contextlib.ExitStack()
    sb = lambda name, shape, dt: ctx.enter_context(nc.sbuf_tensor(name, shape, dt))[:, :]
    psum = lambda name, shape, dt: ctx.enter_context(nc.psum_tensor(name, shape, dt))[:, :]

    with ctx:
        big16 = sb("big16", [128, 65536], BF16)
        WqT = big16[:, 0:NT * 1024]
        WoT = big16[:, NT * 1024:2 * NT * 1024]
        x1fm = big16

        pool = sb("pool", [128, 12288], F32)
        xbuf = [pool[:, 4096:8192], pool[:, 0:4096]]
        stg = [pool[:, 8192:10240], pool[:, 10240:12288]]
        wstg = [pool[:, 8192:10240], pool[:, 10240:12288], pool[:, 0:2048]]
        t0 = pool[:, 8192:9216]
        sqjunk = pool[:, 10240:11264]
        m_st = pool[:, 8192:9216]
        sc_row = pool[0:1, 9216:10240]
        bq_sb = pool[0:1, 11264:12288]
        irow = pool[:, 0:128]
        retr_f32 = pool[:, 9216:10240]
        wfst = pool[:, 0:4096]
        wf0st = pool[:, 8192:12288]
        x1res = pool[:, 4096:6144]
        srmst = pool[:, 6144:8192]
        sfm = [pool[:, 8192:10240], pool[:, 10240:12288]]
        srm = [pool[:, 0:4096], pool[:, 4096:8192]]
        sqjunk_c = pool[:, 8192:9216]

        pbf = sb("pbf", [128, 9216], BF16)
        x_fm = pbf[:, 0:4096]
        sig_rm = pbf[:, 5120:6144]
        sig_fm = pbf[:, 6144:7168]
        retr_rm = pbf[:, 7168:8192]
        retr_fm = pbf[:, 8192:9216]
        wft = [pbf[:, 4096:8192], pbf[:, 0:4096]]

        id_f32 = sb("id_f32", [128, 128], F32)
        ipart = sb("ipart", [128, 1], F32)
        id_bf = sb("id_bf", [128, 128], BF16)
        ones_bf = sb("ones_bf", [1, 128], BF16)
        ones_f32 = sb("ones_f32", [1, 128], F32)
        bq_bf = sb("bq_bf", [1, QD], BF16)
        znat = sb("znat", [H, DH], F32)
        zcol = sb("zcol", [128, H], F32)
        M_aug = sb("M_aug", [128, H * 129], BF16)
        sc_bc = sb("sc_bc", [128, 1024], F32)
        sml = sb("sml", [128, 48], F32)
        accs = sb("accs", [128, 48], F32)
        acc_sum = accs[:, 0:16]
        acc_sq = accs[:, 16:32]
        acc_tmp = accs[:, 32:48]
        eps_sb = sb("eps_sb", [128, 1], F32)
        slots = []
        for par in range(2):
            o = par * 16
            slots.append(dict(
                sum1=sml[:, o + 0:o + 1], mean=sml[:, o + 1:o + 2],
                varr=sml[:, o + 2:o + 3], rstd=sml[:, o + 3:o + 4],
                tmp1=sml[:, o + 4:o + 5], mrs=sml[:, o + 5:o + 6],
                sqp=sml[:, o + 8:o + 12]))
        sum1, mean, varr, rstd, tmp1 = (slots[0][k] for k in
                                        ("sum1", "mean", "varr", "rstd", "tmp1"))
        sqp = slots[0]["sqp"]
        dtmp8 = sml[:, 32:40]
        rden8 = sml[:, 40:48]

        ps_q = [psum("ps_q0", [128, 1024], F32), psum("ps_q1", [128, 1024], F32)]
        ps_r = psum("ps_r", [128, 512], F32)
        ps_b = psum("ps_b", [128, 512], F32)
        ps_pair = [psum("ps_f", [128, 512], F32), psum("ps_g", [128, 512], F32)]
        bank_f, bank_g = ("f", ps_pair[0]), ("g", ps_pair[1])
        bank_r, bank_b = ("r", ps_r), ("b", ps_b)

        rot = {"i": 0, "last": {}, "cycle": None, "ei": 0}

        def set_rot(named_banks, seeds=None):
            rot["i"] = 0
            rot["cycle"] = named_banks
            if seeds:
                for name, op in seeds.items():
                    rot["last"][name] = op

        def tx_group(src_tiles, dst_ap, eng_ev, deps_tx=(), deps_ev=(),
                     ident=None, accum=None, scale_cols=None, tag=""):
            cyc = rot["cycle"]
            bname, pp = cyc[rot["i"] % len(cyc)]
            rot["i"] += 1
            if eng_ev == "alt":
                eng_ev = "scalar" if rot["ei"] % 2 == 0 else "vector"
                rot["ei"] += 1
            idm = ident if ident is not None else id_f32

            def tx_fn(e, pp=pp):
                out = None
                for k, tile in enumerate(src_tiles):
                    if accum:
                        for rr, t in enumerate(tile):
                            out = e.matmul(pp[:, k * 128:(k + 1) * 128], t, idm,
                                           is_transpose=True, start=(rr == 0),
                                           stop=(rr == len(tile) - 1))
                    else:
                        out = e.matmul(pp[:, k * 128:(k + 1) * 128], tile, idm,
                                       is_transpose=True, start=True, stop=True)
                return out
            otx = P.add("tensor", tx_fn,
                        deps=list(deps_tx) + [rot["last"].get(bname)], tag=f"{tag}_tx")
            n = len(src_tiles)
            if scale_cols is None:
                def ev_fn(e, pp=pp, n=n, eng_ev=eng_ev):
                    if eng_ev == "scalar":
                        return e.copy(dst_ap, pp[:, 0:n * 128])
                    return e.tensor_copy(dst_ap, pp[:, 0:n * 128])
            else:
                def ev_fn(e, pp=pp, eng_ev=eng_ev):
                    last = None
                    for k, (dcol, scol) in enumerate(zip(dst_ap, scale_cols)):
                        src = pp[:, k * 128:(k + 1) * 128]
                        if eng_ev == "vector":
                            last = e.tensor_scalar(dcol, src, scol, None, OP.mult)
                        else:
                            last = e.activation(dcol, src, AF.Copy, scale=scol)
                    return last
            oev = P.add(eng_ev, ev_fn, deps=[otx] + list(deps_ev), tag=f"{tag}_ev")
            rot["last"][bname] = oev
            return otx, oev

        o_bq = P.add("sync", lambda e: e.dma_start(bq_sb[0:1, :], bq_d[None, :]),
                     dma=True, key="bq", tag="bq")
        o_z = P.add("sync", lambda e: e.dma_start(znat[:, :], z_d[:, :]),
                    dma=True, key="z", tag="z")
        o_m = P.add("sync", lambda e: e.dma_start(
            m_st.rearrange("p (h e) -> p h e", e=DH),
            m_d[:, :, :].transpose([1, 0, 2])), dma=True, key="m", tag="m")
        o_sc = P.add("sync", lambda e: e.dma_start(
            sc_row[0:1, :].rearrange("p (a b) -> p a b", b=32),
            sc_d[None, :, :]), dma=True, key="sc", tag="sc")

        o_io1 = P.add("gpsimd", lambda e: e.iota(
            irow, [[1, 128]], base=0, channel_multiplier=0,
            allow_small_or_imprecise_dtypes=True), tag="iota_row")
        o_io2 = P.add("gpsimd", lambda e: e.iota(
            ipart[:, 0:1], [[1, 1]], base=0, channel_multiplier=1,
            allow_small_or_imprecise_dtypes=True), tag="iota_part")
        o_id = P.add("vector", lambda e: e.tensor_scalar(
            id_f32, irow, ipart[:, 0:1], None, OP.is_equal),
            deps=[o_io1, o_io2], tag="id_f32")
        o_idb = P.add("vector", lambda e: e.tensor_copy(id_bf, id_f32),
                      deps=[o_id], tag="id_bf")
        o_ones = P.add("gpsimd", lambda e: e.memset(ones_bf[0:1, :], 1.0), tag="ones")
        o_onesf = P.add("gpsimd", lambda e: e.memset(ones_f32[0:1, :], 1.0), tag="ones_f32")
        o_eps = P.add("gpsimd", lambda e: e.memset(eps_sb[:, 0:1], 1e-5), tag="eps")
        o_bqb = P.add("vector", lambda e: e.tensor_copy(bq_bf[0:1, :], bq_sb[0:1, :]),
                      deps=[o_bq], tag="bq_bf")
        o_zt = P.add("tensor", lambda e: e.transpose(ps_r[:, 0:H], znat[0:H, :],
                                                     id_f32[0:H, 0:H]),
                     deps=[o_z, o_id], tag="z_T")
        o_zc = P.add("vector", lambda e: e.tensor_copy(zcol[:, :], ps_r[:, 0:H]),
                     deps=[o_zt], tag="zcol")
        o_ma = P.add("vector", lambda e: e.tensor_copy(
            M_aug.rearrange("p (h c) -> p h c", c=129)[:, :, 0:DH],
            m_st.rearrange("p (h e) -> p h e", e=DH)), deps=[o_m], tag="M_aug")
        o_mz = P.add("vector", lambda e: e.tensor_copy(
            M_aug.rearrange("p (h c) -> p h c", c=129)[:, :, 128:129],
            zcol.rearrange("p (h o) -> p h o", o=1)), deps=[o_zc, o_ma], tag="M_aug_z")
        o_scmm = P.add("tensor", lambda e: [
            e.matmul(ps_q[0][:, hh * 512:(hh + 1) * 512], ones_f32[0:1, 0:128],
                     sc_row[0:1, hh * 512:(hh + 1) * 512], start=True, stop=True)
            for hh in range(2)],
            deps=[o_sc, o_onesf], tag="sc_mm")
        o_scb = P.add("vector", lambda e: e.tensor_copy(sc_bc[:, :], ps_q[0][:, 0:1024]),
                      deps=[o_scmm], tag="sc_bc")

        set_rot([bank_f, bank_g])
        slot = [pool[:, k * 2048:(k + 1) * 2048] for k in range(6)]
        slot_last = [None] * 6
        _wcnt = [0]

        def w_engine():
            n = _wcnt[0]
            _wcnt[0] += 1
            if n >= 54:
                return "gpsimd", f"gw{n}"
            return ("sync", "scalar")[n % 2], None

        def w_load(sl, srcap, deps, tag):
            weng, wkey = w_engine()
            extra = []
            if slot_last[sl] is None:
                if sl == 0:
                    extra = [o_id]
                elif sl == 4:
                    extra = [o_ma, o_scb]
                elif sl == 5:
                    extra = [o_bqb]
            od = P.add(weng, lambda e, sl=sl, srcap=srcap: e.dma_start(
                slot[sl], srcap),
                deps=[slot_last[sl]] + extra + list(deps),
                dma=True, key=wkey or f"w{sl}", tag=tag)
            return od

        wq_all = []
        for g in range(16):
            wq_all.append(P.add("gpsimd", lambda e, g=g: e.dma_start(
                WqT[:, g * 2048:(g + 1) * 2048].rearrange(
                    "p (l c) -> p l c", l=2),
                wq_d[2 * g:2 * g + 2].transpose([1, 0, 2])),
                dma=True, key=f"wqg{g}", tag=f"wq_dma{g}"))
        wq_ready = P.add("gpsimd", lambda e: e.memset(ipart[:, 0:1], 1.0),
                         deps=wq_all + [o_id], tag="wq_bar")
        wq_ready2 = None
        xt0_op = P.add("gpsimd", lambda e: e.dma_start(
            x_fm.rearrange("p (it r) -> p it r", r=128), xt_d[0]),
            dma=True, key="xt0", tag="A_xfm0")

        add3s = [None, None]
        for fold in range(16):
            ht, hf = fold // 2, fold % 2
            st3 = fold % 2
            a, b, c = (0, 1, 2) if st3 == 0 else (3, 4, 5)
            cols = slice(hf * 2048, (hf + 1) * 2048)
            lds = []
            for r, sl in ((0, a), (1, b), (2, c)):
                lds.append(w_load(sl, wo_d[ht, r][:, cols], [], f"wo{fold}_{r}"))
            oa1 = P.add("vector", lambda e, a=a, b=b: e.tensor_add(
                slot[a], slot[a], slot[b]),
                deps=[lds[0], lds[1]], tag=f"wo_a1_{fold}")
            slot_last[b] = oa1
            ld3 = w_load(b, wo_d[ht, 3][:, cols], [], f"wo{fold}_3")
            oa2 = P.add("vector", lambda e, a=a, c=c: e.tensor_add(
                slot[a], slot[a], slot[c]),
                deps=[lds[2], oa1], tag=f"wo_a2_{fold}")
            slot_last[c] = oa2
            add3 = P.add("vector", lambda e, ht=ht, hf=hf, a=a, b=b: e.tensor_add(
                WoT[:, ht * D + hf * 2048: ht * D + (hf + 1) * 2048],
                slot[a], slot[b]),
                deps=[oa2, ld3], tag=f"wo_a3_{fold}")
            slot_last[a] = add3
            slot_last[b] = add3
            add3s[st3] = add3
        wo_ready = add3s[0]
        wo_ready2 = add3s[1]

        a_dma = [None] * NS
        a_q = [None] * NS
        a_sig = [None] * NS
        a_txs = [None] * NS
        a_txr = [None] * NS
        a_t4 = [None] * NS
        a_spl = [[None, None] for _ in range(NS)]
        a_res_last = [None] * NS
        a_var = [None] * NS
        a_nrm_v = [None] * NS
        a_nrm_s = [None] * NS
        a_osq = [None] * NS
        a_dmul_last = [None] * NS
        a_resr_last = [None] * NS
        a_xev = [None] * NS
        last_psb = [None]
        last_psr = [o_zc]

        def ln_chain(tagp, sq_ops, osum, onr_prev, osq_prev, sl):
            base = [o for o in onr_prev if o is not None]
            om = P.add("vector", lambda e: e.tensor_scalar(
                sl["mean"], sl["sum1"], 1.0 / D, None, OP.mult),
                deps=[osum] + base, tag=f"{tagp}_mean")
            or2 = P.add("vector", lambda e: e.reduce_sum(sl["tmp1"], sl["sqp"], AXX),
                        deps=sq_ops + base, tag=f"{tagp}_sqsum")
            ov1 = P.add("vector", lambda e: e.tensor_scalar(
                sl["varr"], sl["tmp1"], 1.0 / D, None, OP.mult),
                deps=[or2, osq_prev], tag=f"{tagp}_var1")
            ov2 = P.add("vector", lambda e: e.tensor_tensor(
                sl["tmp1"], sl["mean"], sl["mean"], OP.mult),
                deps=[om, ov1], tag=f"{tagp}_msq")
            ov3 = P.add("vector", lambda e: e.tensor_sub(sl["varr"], sl["varr"], sl["tmp1"]),
                        deps=[ov1, ov2], tag=f"{tagp}_var")
            return ov3

        def emit_xdma(s):
            if s >= NS:
                return
            a_dma[s] = P.add("scalar", lambda e, s=s: e.dma_start(
                xbuf[s % 2], x_d[s * 128:(s + 1) * 128, :]),
                deps=[a_t4[s - 2] if s >= 2 else None,
                      wo_ready if s <= 1 else None,
                      wo_ready2 if s <= 1 else None],
                dma=True, key=f"x{s%2}", tag=f"A_xdma{s}")

        def emit_xtx(s):
            if s == 0:
                a_xev[0] = [xt0_op]
                return
            a_xev[s] = [P.add("gpsimd", lambda e, s=s: e.dma_start(
                x_fm.rearrange("p (it r) -> p it r", r=128),
                xt_d[s]),
                deps=[a_q[s - 1]],
                dma=True, key=f"xt{s}", tag=f"A_xfm{s}")]

        def emit_q(s):
            def q_fn(e, s=s):
                last = None
                for half in range(2):
                    dst = ps_q[s % 2][:, half * 512:(half + 1) * 512]
                    for it in range(NT):
                        e.matmul(dst, x_fm[:, it * 128:(it + 1) * 128],
                                 WqT[:, it * 1024 + half * 512: it * 1024 + half * 512 + 512],
                                 start=(it == 0), stop=False)
                    last = e.matmul(dst, ones_bf[0:1, 0:128],
                                    bq_bf[0:1, half * 512:(half + 1) * 512],
                                    start=False, stop=True)
                return last
            a_q[s] = P.add("tensor", q_fn,
                           deps=a_xev[s] + [o_ones, o_bqb, wq_ready, wq_ready2,
                                            o_scb if s == 0 else None,
                                            a_dmul_last[s - 1] if s >= 1 else None],
                           tag=f"A_q{s}")

        def backQ1(s):
            oexp = P.add("scalar", lambda e, s=s: e.activation(
                t0, ps_q[s % 2][:, :], AF.Exp),
                deps=[a_q[s],
                      a_spl[s - 1][0] if s >= 1 else wo_ready,
                      None if s >= 1 else wo_ready2],
                tag=f"A_exp{s}")
            orel = P.add("vector", lambda e, s=s: e.tensor_scalar(
                retr_f32, ps_q[s % 2][:, :], 0.0, None, OP.max),
                deps=[a_q[s], oexp,
                      a_txr[s - 1] if s >= 1 else None,
                      a_spl[s - 1][0] if s >= 1 else wo_ready,
                      None if s >= 1 else wo_ready2],
                tag=f"A_rel{s}")
            a_sig[s] = P.add("vector", lambda e: e.scalar_tensor_tensor(
                t0, t0, 1.0, retr_f32, OP.min, OP.add),
                deps=[oexp, orel], tag=f"A_sig{s}")
            sev = []
            for g in range(2):
                tiles = [t0[:, (g * 4 + k) * 128:(g * 4 + k + 1) * 128]
                         for k in range(4)]
                otx, oev = tx_group(tiles, sig_fm[:, g * 512:(g + 1) * 512], "alt",
                                    deps_tx=[a_sig[s]],
                                    tag=f"A_st{s}_{g}")
                sev.append(oev)
                a_txs[s] = otx
            psq_o = ps_q[(s + 1) % 2]
            rslot = [(psq_o, 0), (psq_o, 129), (psq_o, 258),
                     (psq_o, 512), (psq_o, 641), (psq_o, 770),
                     (ps_r, 0), (ps_r, 129)]
            omm = P.add("tensor", lambda e, rslot=rslot: [
                e.matmul(rslot[h][0][:, rslot[h][1]:rslot[h][1] + 129],
                         sig_fm[:, h * 128:(h + 1) * 128],
                         M_aug[:, h * 129:(h + 1) * 129],
                         start=True, stop=True)
                for h in range(8)],
                deps=[sev[0], sev[1], o_mz, last_psr[0],
                      a_sig[s - 1] if s >= 1 else None,
                      a_res_last[s - 1] if s >= 1 else None,
                      a_resr_last[s - 1] if s >= 1 else None,
                      o_scb if s == 0 else None],
                tag=f"A_ret{s}")
            den_aps = [
                (psq_o[:, 128:515].rearrange("p (e c) -> p e c", c=129)[:, :, 0:1], 0, 3),
                (psq_o[:, 640:898].rearrange("p (e c) -> p e c", c=129)[:, :, 0:1], 3, 2),
                (psq_o[:, 898:899].rearrange("p (e c) -> p e c", c=1), 5, 1),
                (ps_r[:, 128:386].rearrange("p (e c) -> p e c", c=129)[:, :, 0:1], 6, 2),
            ]
            dd = []
            for k, (dn, off, cnt) in enumerate(den_aps):
                dd.append(P.add("vector", lambda e, dn=dn, off=off, cnt=cnt: e.tensor_scalar(
                    dtmp8[:, off:off + cnt].rearrange("p (e c) -> p e c", c=1),
                    dn, 1e-6, None, OP.add),
                    deps=[omm,
                          a_dmul_last[s - 1] if (s >= 1 and k == 0) else None],
                    tag=f"A_dden{s}_{k}"))
            orcp = P.add("vector", lambda e: e.reciprocal(rden8, dtmp8),
                         deps=dd, tag=f"A_drcp{s}")
            omul = None
            for h in range(8):
                omul = P.add("vector", lambda e, h=h, rslot=rslot: e.tensor_scalar(
                    retr_f32[:, h * 128:(h + 1) * 128],
                    rslot[h][0][:, rslot[h][1]:rslot[h][1] + 128],
                    rden8[:, h:h + 1], None, OP.mult),
                    deps=[orcp, a_sig[s] if h == 0 else None],
                    tag=f"A_dmul{s}_{h}")
            a_dmul_last[s] = omul
            last_psr[0] = omul

        def backQ2(s):
            prev_div = a_dmul_last[s]
            rev = []
            for g in range(2):
                tiles = [retr_f32[:, (g * 4 + k) * 128:(g * 4 + k + 1) * 128]
                         for k in range(4)]
                otx, oev = tx_group(tiles, retr_fm[:, g * 512:(g + 1) * 512], "alt",
                                    deps_tx=[prev_div],
                                    tag=f"A_rt{s}_{g}")
                rev.append(oev)
                a_txr[s] = otx
            res_ops = []
            for oc in range(8):
                pdst = (ps_b, ps_q[s % 2][:, 0:512], ps_r)[oc % 3]
                first_dep = (last_psb[0] if oc == 0 else
                             a_sig[s] if oc == 1 else
                             prev_div)
                omm = P.add("tensor", lambda e, oc=oc, pdst=pdst: [
                    e.matmul(pdst, retr_fm[:, ht * 128:(ht + 1) * 128],
                             WoT[:, ht * D + oc * 512: ht * D + oc * 512 + 512],
                             start=(ht == 0), stop=(ht == 7))
                    for ht in range(8)],
                    deps=[rev[0], rev[1], wo_ready, wo_ready2,
                          res_ops[oc - 3] if oc >= 3 else first_dep],
                    tag=f"A_att{s}_{oc}")
                ores = P.add("vector", lambda e, s=s, oc=oc, pdst=pdst: e.tensor_add(
                    xbuf[s % 2][:, oc * 512:(oc + 1) * 512], pdst,
                    xbuf[s % 2][:, oc * 512:(oc + 1) * 512]),
                    deps=[omm, a_dma[s] if oc == 0 else None],
                    tag=f"A_res{s}_{oc}")
                res_ops.append(ores)
            last_psb[0] = res_ops[6]
            a_resr_last[s] = res_ops[5]
            a_res_last[s] = res_ops[-1]
            osum = P.add("vector", lambda e, s=s: e.reduce_sum(sum1, xbuf[s % 2][:, :], AXX),
                         deps=[res_ops[-1], a_nrm_v[s - 1] if s >= 1 else None,
                               a_nrm_s[s - 1] if s >= 1 else None],
                         tag=f"A_sum{s}")
            sq_ops = []
            for g in range(4):
                sq_ops.append(P.add("scalar", lambda e, s=s, g=g: e.activation(
                    sqjunk, xbuf[s % 2][:, g * 1024:(g + 1) * 1024], AF.Square,
                    accum_out=sqp[:, g:g + 1]),
                    deps=[res_ops[-1], a_var[s - 1] if s >= 1 else None,
                          sq_ops[-1] if g > 0 else None,
                          (a_spl[s - 1][1] if s >= 1 else wo_ready) if g == 0 else None,
                          (None if s >= 1 else wo_ready2) if g == 0 else None],
                    tag=f"A_sq{s}_{g}"))
            a_var[s] = ln_chain(f"A{s}", sq_ops, osum,
                                [a_nrm_v[s - 1], a_nrm_s[s - 1]] if s >= 1 else [],
                                a_osq[s - 1] if s >= 1 else None, slots[0])
            osq = P.add("scalar", lambda e: e.activation(tmp1, varr, AF.Sqrt,
                                                         bias=eps_sb[:, 0:1]),
                        deps=[a_var[s], o_eps], tag=f"A_sqrt{s}")
            a_osq[s] = osq
            orc = P.add("vector", lambda e: e.reciprocal(rstd, tmp1),
                        deps=[osq, a_nrm_v[s - 1] if s >= 1 else None,
                              a_nrm_s[s - 1] if s >= 1 else None], tag=f"A_rcp{s}")
            omrs = P.add("vector", lambda e: e.scalar_tensor_tensor(
                slots[0]["mrs"], mean, -1.0, rstd, OP.mult, OP.mult),
                deps=[orc, a_nrm_s[s - 1] if s >= 1 else None], tag=f"A_mrs{s}")
            a_nrm_v[s] = P.add("vector", lambda e, s=s: e.tensor_scalar(
                xbuf[s % 2][:, 0:2048], xbuf[s % 2][:, 0:2048],
                mean[:, 0:1], rstd[:, 0:1],
                OP.subtract, OP.mult),
                deps=[orc], tag=f"A_nrmv{s}")
            a_nrm_s[s] = P.add("scalar", lambda e, s=s: e.activation(
                xbuf[s % 2][:, 2048:4096], xbuf[s % 2][:, 2048:4096], AF.Identity,
                bias=slots[0]["mrs"][:, 0:1], scale=rstd[:, 0:1]),
                deps=[omrs], tag=f"A_nrms{s}")

        def t4part(s):
            t4e = {}
            for g in range(8):
                tiles = [xbuf[s % 2][:, (g * 4 + k) * 128:(g * 4 + k + 1) * 128]
                         for k in range(4)]
                otx, oev = tx_group(
                    tiles, stg[g // 4][:, (g % 4) * 512:((g % 4) + 1) * 512], "alt",
                    deps_tx=[a_nrm_v[s] if g < 4 else a_nrm_s[s]],
                    deps_ev=[a_spl[s - 1][g // 4] if s >= 1 else None],
                    tag=f"A_t4_{s}_{g}")
                t4e[g] = oev
                if g == 7:
                    a_t4[s] = otx
            for hf in range(2):
                a_spl[s][hf] = P.add("scalar", lambda e, s=s, hf=hf: e.dma_start(
                    scr1[s, :, hf * 16:(hf + 1) * 16, :],
                    stg[hf].rearrange("p (it r) -> p it r", r=128)),
                    deps=[t4e[4 * hf + gg] for gg in range(4)],
                    dma=True, key=f"spl{hf}", tag=f"A_spl{s}_{hf}")

        emit_xdma(0)
        emit_xdma(1)
        emit_xtx(0)
        emit_q(0)
        backQ1(0)
        for s in range(1, NS):
            emit_xtx(s)
            backQ2(s - 1)
            emit_q(s)
            t4part(s - 1)
            emit_xdma(s + 1)
            backQ1(s)
        backQ2(NS - 1)
        t4part(NS - 1)

        a_all_sp0 = a_spl[NS - 1][0]
        a_all_sp1 = a_spl[NS - 1][1]
        p2_first_deps = [a_q[NS - 1], a_res_last[NS - 1], a_sig[NS - 1]]

        owd0a = P.add("sync", lambda e: e.dma_start(
            wf0st[:, 0:2048], wf_d[0][:, 0:2048]),
            deps=[a_all_sp0, a_all_sp1], dma=True, key="wf", tag="B_wdma0a")
        owd0b = P.add("scalar", lambda e: e.dma_start(
            wf0st[:, 2048:4096], wf_d[0][:, 2048:4096]),
            deps=[a_all_sp0, a_all_sp1], dma=True, key="wf0b", tag="B_wdma0b")
        evs0 = []
        for g in range(8):
            eng_pick = "vector" if g % 2 == 0 else "scalar"

            def dq0_fn(e, g=g, eng_pick=eng_pick):
                last = None
                for k in range(4):
                    it = g * 4 + k
                    dst = wft[0][:, it * 128:(it + 1) * 128]
                    srcc = wf0st[:, it * 128:(it + 1) * 128]
                    scal = sc_bc[:, it:it + 1]
                    if eng_pick == "vector":
                        last = e.tensor_scalar(dst, srcc, scal, None, OP.mult)
                    else:
                        last = e.activation(dst, srcc, AF.Copy, scale=scal)
                return last
            evs0.append(P.add(eng_pick, dq0_fn,
                              deps=[owd0a if g < 4 else owd0b, o_scb],
                              tag=f"B_dq0_{g}"))

        nsh = 3 if NS >= 3 else 1
        shsz = (NS + nsh - 1) // nsh
        pair_dst = [pool[:, 0:4096], pool[:, 4096:8192]]
        pair_p2 = {}
        p2_ops = []
        for pr in range(NT // 2):
            it0 = 2 * pr
            dst = pair_dst[pr % 2]
            halves = []
            for sh in range(nsh):
                lo, hi = sh * shsz, min(NS, (sh + 1) * shsz)
                if lo >= hi:
                    continue
                eng3 = ("sync", "scalar", "gpsimd")[sh % 3]
                key3 = f"p1g{pr}" if eng3 == "gpsimd" else f"p1_{pr % 2}_{sh}"
                od = P.add(eng3,
                           lambda e, it0=it0, lo=lo, hi=hi, dst=dst: e.dma_start(
                    dst.rearrange("p (s l r) -> p s l r", l=2, r=128)[:, lo:hi],
                    scr1[:, :, it0:it0 + 2, :].transpose([1, 0, 2, 3])[:, lo:hi]),
                    deps=[a_all_sp0, a_all_sp1,
                          pair_p2.get(pr - 2),
                          a_t4[NS - 2] if (pr <= 1 and NS >= 2) else None,
                          a_t4[NS - 1] if pr <= 1 else None],
                    dma=True, key=key3, tag=f"P1_{pr}_{sh}")
                halves.append(od)
            lastp = None
            for itl in range(2):
                lastp = P.add("vector",
                              lambda e, it0=it0, itl=itl, dst=dst: e.tensor_copy(
                    x1fm[:, (it0 + itl) * R:(it0 + itl + 1) * R],
                    dst.rearrange("p (s l r) -> p s l r", l=2, r=128)[:, 0:NS, itl, :]),
                    deps=halves + (p2_first_deps if pr == 0 else []),
                    tag=f"P2_{it0 + itl}")
                p2_ops.append(lastp)
            pair_p2[pr] = lastp
        x1fm_ready = pair_p2[NT // 2 - 1]
        p2_last = [pair_p2[NT // 2 - 1], pair_p2[NT // 2 - 2]]

        set_rot([bank_f, bank_g, bank_r, bank_b],
                seeds={"r": a_resr_last[NS - 1], "b": last_psb[0]})
        NN = max(1, R // 512)
        nw = min(512, R)
        btx_last = {}
        btx2_last = {}
        b_stat = [None]
        bjunk = pbf[:, 8192:9216]
        b_mm = [None] * NT
        b_sf = [None] * NT
        b_t6 = [None] * NT
        b_spl = [None] * NT

        def emit_bt6(oc):
            ngr = (NS + 3) // 4
            ev6 = []
            for g in range(ngr):
                kk = min(4, NS - g * 4)
                tiles = [sfm[oc % 2][:, (g * 4 + k) * 128:(g * 4 + k + 1) * 128]
                         for k in range(kk)]
                otx, oev = tx_group(
                    tiles, srmst[:, g * 512:g * 512 + kk * 128], "scalar",
                    deps_tx=[b_sf[oc]],
                    deps_ev=[b_spl[oc - 1] if (oc >= 1 and g == 0) else None,
                             b_stat[0] if (oc >= 1 and g == 0) else None],
                    tag=f"B_s6{oc}_{g}")
                if g == ngr - 1:
                    b_t6[oc] = otx
                ev6.append(oev)
            b_spl[oc] = P.add("scalar", lambda e, oc=oc: e.dma_start(
                scr2[:, :, oc, :].transpose([1, 0, 2]),
                srmst[:, 0:NS * 128].rearrange("p (s o) -> p s o", o=128)),
                deps=ev6, dma=True, key="bspl", tag=f"B_spl{oc}")
            sr3 = srmst[:, 0:NS * 128].rearrange("p (s o) -> p s o", o=128)
            o_rs = P.add("vector", lambda e: e.reduce_sum(
                acc_tmp[:, 0:NS], sr3, AXX), deps=ev6 + [b_stat[0]], tag=f"B_rs{oc}")
            if oc == 0:
                o_as = P.add("vector", lambda e: e.tensor_copy(
                    acc_sum[:, 0:NS], acc_tmp[:, 0:NS]),
                    deps=[o_rs, a_nrm_v[NS - 1], a_nrm_s[NS - 1]], tag=f"B_as{oc}")
            else:
                o_as = P.add("vector", lambda e: e.tensor_add(
                    acc_sum[:, 0:NS], acc_sum[:, 0:NS], acc_tmp[:, 0:NS]),
                    deps=[o_rs], tag=f"B_as{oc}")
            prev = o_as
            half = (NS + 1) // 2
            for hh in range(2):
                lo, hi = hh * half, min(NS, (hh + 1) * half)
                if lo >= hi:
                    continue
                osq = P.add("vector", lambda e, lo=lo, hi=hi: e.tensor_tensor(
                    bjunk[:, 0:(hi - lo) * 128],
                    srmst[:, lo * 128:hi * 128], srmst[:, lo * 128:hi * 128],
                    OP.mult), deps=[prev], tag=f"B_sq{oc}_{hh}")
                orr = P.add("vector", lambda e, lo=lo, hi=hi: e.reduce_sum(
                    acc_tmp[:, lo:hi],
                    bjunk[:, 0:(hi - lo) * 128].rearrange("p (s o) -> p s o", o=128),
                    AXX), deps=[osq], tag=f"B_rq{oc}_{hh}")
                if oc == 0:
                    prev = P.add("vector", lambda e, lo=lo, hi=hi: e.tensor_copy(
                        acc_sq[:, lo:hi], acc_tmp[:, lo:hi]),
                        deps=[orr], tag=f"B_aq{oc}_{hh}")
                else:
                    prev = P.add("vector", lambda e, lo=lo, hi=hi: e.tensor_add(
                        acc_sq[:, lo:hi], acc_sq[:, lo:hi], acc_tmp[:, lo:hi]),
                        deps=[orr], tag=f"B_aq{oc}_{hh}")
            b_stat[0] = prev

        for oc in range(NT):
            if oc == 0:
                evs = evs0
            else:
                owd = P.add("sync", lambda e, oc=oc: e.dma_start(
                    wfst, wf_d[oc]),
                    deps=[btx_last[oc - 1], btx2_last[oc - 1]] +
                         ([pair_p2[NT // 2 - 2]] if oc == 1 else []),
                    dma=True, key="wf", tag=f"B_wdma{oc}")
                evs = []
                for g in range(8):
                    eng_pick = "vector" if g % 2 == 0 else "scalar"

                    def dq_fn(e, oc=oc, g=g, eng_pick=eng_pick):
                        last = None
                        for k in range(4):
                            it = g * 4 + k
                            dst = wft[oc % 2][:, it * 128:(it + 1) * 128]
                            srcc = wfst[:, it * 128:(it + 1) * 128]
                            scal = sc_bc[:, oc * 32 + it:oc * 32 + it + 1]
                            if eng_pick == "vector":
                                last = e.tensor_scalar(dst, srcc, scal, None, OP.mult)
                            else:
                                last = e.activation(dst, srcc, AF.Copy, scale=scal)
                        return last
                    oev = P.add(eng_pick, dq_fn,
                                deps=[owd, o_scb, b_mm[oc - 2] if oc >= 2 else None],
                                tag=f"B_dq{oc}_{g}")
                    evs.append(oev)
            btx_last[oc] = evs[-1]
            btx2_last[oc] = evs[-2]

            if oc >= 1:
                emit_bt6(oc - 1)

            def mm_fn(e, oc=oc, itlist=None):
                last = None
                for it in itlist:
                    for nn in range(NN):
                        t = ps_q[0] if nn < 2 else ps_q[1]
                        dst = t[:, (nn % 2) * nw:(nn % 2) * nw + nw]
                        last = e.matmul(dst, wft[oc % 2][:, it * 128:(it + 1) * 128],
                                        x1fm[:, it * R + nn * nw: it * R + nn * nw + nw],
                                        start=(it == 0), stop=(it == NT - 1))
                return last
            base_deps = [b_sf[oc - 1] if oc >= 1 else None,
                         a_dmul_last[NS - 1] if oc == 0 else None,
                         a_res_last[NS - 1] if oc == 0 else None]
            if oc <= 1:
                last_mm = None
                for kq in range(4):
                    itl = list(range(kq * 8, (kq + 1) * 8))
                    last_mm = P.add(
                        "tensor",
                        lambda e, oc=oc, itl=itl: mm_fn(e, oc=oc, itlist=itl),
                        deps=evs + base_deps + [p2_ops[itl[-1]], last_mm],
                        tag=f"B_mm{oc}_{kq}")
                b_mm[oc] = last_mm
            else:
                b_mm[oc] = P.add("tensor",
                                 lambda e, oc=oc: mm_fn(e, oc=oc, itlist=list(range(NT))),
                                 deps=evs + base_deps, tag=f"B_mm{oc}")
            ord_ = P.add("sync", lambda e, oc=oc: e.dma_start(
                x1res[:, 0:NS * 128].rearrange("p (s r) -> p s r", r=128),
                scr1[:, :, oc, :].transpose([1, 0, 2])),
                deps=[a_all_sp0, a_all_sp1, b_sf[oc - 1] if oc >= 1 else None,
                      pair_p2[NT // 2 - 1] if oc == 0 else None],
                dma=True, key="bres", tag=f"B_res{oc}")

            def sf_fn(e, oc=oc):
                if R >= 2048:
                    e.tensor_add(sfm[oc % 2][:, 0:1024], ps_q[0][:, :], x1res[:, 0:1024])
                    return e.tensor_add(sfm[oc % 2][:, 1024:2048], ps_q[1][:, :],
                                        x1res[:, 1024:2048])
                return e.tensor_add(sfm[oc % 2][:, 0:R], ps_q[0][:, 0:R], x1res[:, 0:R])
            b_sf[oc] = P.add("vector", sf_fn,
                             deps=[b_mm[oc], ord_, b_t6[oc - 2] if oc >= 2 else None],
                             tag=f"B_sf{oc}")
        emit_bt6(NT - 1)

        c_out = []
        stat_fin = b_stat[0]
        cst = sb("cst", [128, 80], F32)
        mean16 = cst[:, 0:16]
        varr16 = cst[:, 16:32]
        tmp16 = cst[:, 32:48]
        rstd16 = cst[:, 48:64]
        mrs16 = cst[:, 64:80]
        som = P.add("vector", lambda e: e.tensor_scalar(
            mean16[:, 0:NS], acc_sum[:, 0:NS], 1.0 / D, None, OP.mult),
            deps=[stat_fin], tag="C_mean16")
        sov1 = P.add("vector", lambda e: e.tensor_scalar(
            varr16[:, 0:NS], acc_sq[:, 0:NS], 1.0 / D, None, OP.mult),
            deps=[stat_fin], tag="C_var116")
        sov2 = P.add("vector", lambda e: e.tensor_tensor(
            tmp16[:, 0:NS], mean16[:, 0:NS], mean16[:, 0:NS], OP.mult),
            deps=[som], tag="C_msq16")
        sov3 = P.add("vector", lambda e: e.tensor_sub(
            varr16[:, 0:NS], varr16[:, 0:NS], tmp16[:, 0:NS]),
            deps=[sov1, sov2], tag="C_var16")
        sosq = P.add("scalar", lambda e: e.activation(
            tmp16[:, 0:NS], varr16[:, 0:NS], AF.Sqrt, bias=eps_sb[:, 0:1]),
            deps=[sov3, o_eps], tag="C_sqrt16")
        sorc = P.add("vector", lambda e: e.reciprocal(
            rstd16[:, 0:NS], tmp16[:, 0:NS]), deps=[sosq], tag="C_rcp16")
        somrs = P.add("vector", lambda e: e.scalar_tensor_tensor(
            mrs16[:, 0:NS], mean16[:, 0:NS], -1.0, rstd16[:, 0:NS],
            OP.mult, OP.mult), deps=[sorc], tag="C_mrs16")
        srm3 = [pool[:, 0:4096], pool[:, 4096:8192], pool[:, 8192:12288]]
        for s in range(NS):
            sl3 = srm3[s % 3]
            ieng = ("sync", "scalar", "gpsimd")[s % 3]
            ikey = f"cing{s}" if ieng == "gpsimd" else f"cin{s % 3}"
            od = P.add(ieng, lambda e, s=s, sl3=sl3: e.dma_start(sl3, scr2[s]),
                       deps=[b_spl[NT - 1], stat_fin,
                             c_out[s - 3] if s >= 3 else None],
                       dma=True, key=ikey, tag=f"C_dma{s}")
            if s % 2 == 0:
                onr = P.add("vector", lambda e, s=s, sl3=sl3: e.tensor_scalar(
                    sl3, sl3, mean16[:, s:s + 1], rstd16[:, s:s + 1],
                    OP.subtract, OP.mult),
                    deps=[somrs, od], tag=f"C_nrm{s}")
            else:
                onr = P.add("scalar", lambda e, s=s, sl3=sl3: e.activation(
                    sl3, sl3, AF.Identity,
                    bias=mrs16[:, s:s + 1], scale=rstd16[:, s:s + 1]),
                    deps=[somrs, od], tag=f"C_nrm{s}")
            oeng = ("gpsimd", "sync", "scalar")[s % 3]
            okey = f"cog{s}" if oeng == "gpsimd" else f"cout{s % 3}"
            c_out.append(P.add(oeng, lambda e, s=s, sl3=sl3: e.dma_start(
                y_d[s * 128:(s + 1) * 128, :], sl3),
                deps=[onr], dma=True, key=okey, tag=f"C_out{s}"))

        P.add("vector", lambda e: e.engine_nop(), deps=list(c_out), tag="final_wait")

        P.finalize()
        sems = {}
        for en in ("sync", "scalar", "vector", "tensor", "gpsimd"):
            sems[en] = ctx.enter_context(nc.semaphore(f"sem_{en}"))
        for k in P.keys:
            sems["dma_" + k] = ctx.enter_context(nc.semaphore(f"sem_d_{k}"))
        with nc.Block() as block:
            @block.sync
            def _(eng):
                P.emit("sync", eng, sems)

            @block.scalar
            def _(eng):
                P.emit("scalar", eng, sems)

            @block.vector
            def _(eng):
                P.emit("vector", eng, sems)

            @block.tensor
            def _(eng):
                P.emit("tensor", eng, sems)

            @block.gpsimd
            def _(eng):
                P.emit("gpsimd", eng, sems)

    return nc


def kernel(**inputs):
    x = np.asarray(inputs["L0_attn_hidden_states"], dtype=np.float32)
    Wq = np.asarray(inputs["L0_attn_Wq"], dtype=np.float32)
    bq = np.asarray(inputs["L0_attn_bq"], dtype=np.float32)
    Wo = np.asarray(inputs["L0_attn_Wo"], dtype=np.float32)
    M = np.asarray(inputs["L0_attn_M"], dtype=np.float32)
    z = np.asarray(inputs["L0_attn_z"], dtype=np.float32)
    Wf = np.asarray(inputs["L0_ffn_weight"], dtype=np.float32)
    sc = np.asarray(inputs["L0_ffn_weight_scale"], dtype=np.float32)

    S = x.shape[1]
    R = S // NCORES
    x2 = np.ascontiguousarray(x.reshape(S, D))
    wq_sel = np.ascontiguousarray(
        Wq.reshape(32, 128, 32, 128)[::4]
        .transpose(2, 3, 0, 1)).reshape(NT, 128, QD)
    wo_spl = np.ascontiguousarray(Wo.T.reshape(H, 4, 128, D))
    wf_spl = np.ascontiguousarray(
        Wf.reshape(32, 128, 32, 128).transpose(0, 3, 2, 1)).reshape(NT, 128, D)
    bq_sel = np.ascontiguousarray(bq.reshape(32, 128)[::4].reshape(QD))

    nc = build_nc(R)
    in_maps = []
    for c in range(NCORES):
        xs = x2[c * R:(c + 1) * R]
        in_maps.append({
            "x": np.ascontiguousarray(xs),
            "xt": np.ascontiguousarray(
                xs.reshape(R // 128, 128, D // 128, 128).transpose(0, 3, 2, 1)),
            "wq": wq_sel, "bq": bq_sel, "wo": wo_spl, "m": M, "z": z,
            "wf": wf_spl, "sc": sc,
        })
    res = run_bass_kernel_spmd(nc, in_maps, core_ids=list(range(NCORES)))
    out = np.concatenate([np.asarray(r["y"]) for r in res.results], axis=0)
    return out.reshape(1, S, D).astype(np.float32)


# version 28
# speedup vs baseline: 1.0296x; 1.0296x over previous
import numpy as np
import contextlib
import concourse.bass as bass
import concourse.mybir as mybir
from concourse.bass_utils import run_bass_kernel_spmd

F32 = mybir.dt.float32
BF16 = mybir.dt.bfloat16
AF = mybir.ActivationFunctionType
OP = mybir.AluOpType
AXX = mybir.AxisListType.X

D = 4096
H = 8
DH = 128
QD = H * DH
NT = D // 128
NCORES = 8


class _Op:
    __slots__ = ("eng", "fn", "deps", "dma", "key", "need", "cum", "sem", "tag")

    def __init__(self, eng, fn, deps, dma, key, tag):
        self.eng, self.fn, self.dma, self.key, self.tag = eng, fn, dma, key, tag
        self.deps = [d for d in deps if d is not None]
        self.need = False
        self.cum = None
        self.sem = None


class Plan:

    def __init__(self):
        self.ops = []

    def add(self, eng, fn, deps=(), dma=False, key=None, tag=""):
        assert (key is not None) == dma, tag
        op = _Op(eng, fn, deps, dma, key, tag)
        self.ops.append(op)
        return op

    def finalize(self):
        for op in self.ops:
            if op.dma:
                op.need = True
            for d in op.deps:
                d.need = True
        cums = {}
        self.keys = []
        for op in self.ops:
            if op.dma:
                op.sem = "dma_" + op.key
                if op.key not in self.keys:
                    self.keys.append(op.key)
            else:
                op.sem = op.eng
            if op.need:
                cums[op.sem] = cums.get(op.sem, 0) + (16 if op.dma else 1)
                op.cum = cums[op.sem]
        self.per_engine = {}
        for op in self.ops:
            self.per_engine.setdefault(op.eng, []).append(op)

    def emit(self, ename, eng, sems):
        waited = {}
        for op in self.per_engine.get(ename, []):
            for d in op.deps:
                assert d.cum is not None, f"{op.tag} dep {d.tag} not inc'd"
                if waited.get(d.sem, 0) >= d.cum:
                    continue
                eng.wait_ge(sems[d.sem], d.cum)
                waited[d.sem] = d.cum
            instr = op.fn(eng)
            if isinstance(instr, list):
                instr = instr[-1]
            if op.need:
                assert instr is not None, f"op {op.tag} needs inc but returned None"
                instr.then_inc(sems[op.sem], 16 if op.dma else 1)


def build_nc(R):
    NS = R // 128
    nc = bass.Bass("TRN2", target_bir_lowering=False)

    x_d = nc.dram_tensor("x", [R, D], F32, kind="ExternalInput").ap()
    xt_d = nc.dram_tensor("xt", [R // 128, 128, D // 128, 128], F32,
                          kind="ExternalInput").ap()
    wq_d = nc.dram_tensor("wq", [NT, 128, QD], F32, kind="ExternalInput").ap()
    bq_d = nc.dram_tensor("bq", [QD], F32, kind="ExternalInput").ap()
    wo_d = nc.dram_tensor("wo", [H, 4, 128, D], F32, kind="ExternalInput").ap()
    m_d = nc.dram_tensor("m", [H, DH, DH], F32, kind="ExternalInput").ap()
    z_d = nc.dram_tensor("z", [H, DH], F32, kind="ExternalInput").ap()
    wf_d = nc.dram_tensor("wf", [NT, 128, D], F32, kind="ExternalInput").ap()
    sc_d = nc.dram_tensor("sc", [32, 32], F32, kind="ExternalInput").ap()
    y_d = nc.dram_tensor("y", [R, D], F32, kind="ExternalOutput").ap()
    scr1 = nc.dram_tensor("scr1", [NS, 128, NT, 128], F32, kind="Internal").ap()
    scr2 = nc.dram_tensor("scr2", [NS, 128, NT, 128], F32, kind="Internal").ap()

    P = Plan()
    ctx = contextlib.ExitStack()
    sb = lambda name, shape, dt: ctx.enter_context(nc.sbuf_tensor(name, shape, dt))[:, :]
    psum = lambda name, shape, dt: ctx.enter_context(nc.psum_tensor(name, shape, dt))[:, :]

    with ctx:
        big16 = sb("big16", [128, 65536], BF16)
        WqT = big16[:, 0:NT * 1024]
        WoT = big16[:, NT * 1024:2 * NT * 1024]
        x1fm = big16

        pool = sb("pool", [128, 12288], F32)
        xbuf = [pool[:, 4096:8192], pool[:, 0:4096]]
        stg = [pool[:, 8192:10240], pool[:, 10240:12288]]
        wstg = [pool[:, 8192:10240], pool[:, 10240:12288], pool[:, 0:2048]]
        t0 = pool[:, 8192:9216]
        sqjunk = pool[:, 10240:11264]
        m_st = pool[:, 8192:9216]
        sc_row = pool[0:1, 9216:10240]
        bq_sb = pool[0:1, 11264:12288]
        irow = pool[:, 0:128]
        retr_f32 = pool[:, 9216:10240]
        wfst = pool[:, 0:4096]
        wf0st = pool[:, 8192:12288]
        x1res = pool[:, 4096:6144]
        srmst = pool[:, 6144:8192]
        sfm = [pool[:, 8192:10240], pool[:, 10240:12288]]
        srm = [pool[:, 0:4096], pool[:, 4096:8192]]
        sqjunk_c = pool[:, 8192:9216]

        pbf = sb("pbf", [128, 9216], BF16)
        x_fm = pbf[:, 0:4096]
        sig_rm = pbf[:, 5120:6144]
        sig_fm = pbf[:, 6144:7168]
        retr_rm = pbf[:, 7168:8192]
        retr_fm = pbf[:, 8192:9216]
        wft = [pbf[:, 4096:8192], pbf[:, 0:4096]]

        id_f32 = sb("id_f32", [128, 128], F32)
        ipart = sb("ipart", [128, 1], F32)
        id_bf = sb("id_bf", [128, 128], BF16)
        ones_bf = sb("ones_bf", [1, 128], BF16)
        ones_f32 = sb("ones_f32", [1, 128], F32)
        bq_bf = sb("bq_bf", [1, QD], BF16)
        znat = sb("znat", [H, DH], F32)
        zcol = sb("zcol", [128, H], F32)
        M_aug = sb("M_aug", [128, H * 129], BF16)
        sc_bc = sb("sc_bc", [128, 1024], F32)
        sml = sb("sml", [128, 48], F32)
        accs = sb("accs", [128, 48], F32)
        acc_sum = accs[:, 0:16]
        acc_sq = accs[:, 16:32]
        acc_tmp = accs[:, 32:48]
        eps_sb = sb("eps_sb", [128, 1], F32)
        slots = []
        for par in range(2):
            o = par * 16
            slots.append(dict(
                sum1=sml[:, o + 0:o + 1], mean=sml[:, o + 1:o + 2],
                varr=sml[:, o + 2:o + 3], rstd=sml[:, o + 3:o + 4],
                tmp1=sml[:, o + 4:o + 5], mrs=sml[:, o + 5:o + 6],
                sqp=sml[:, o + 8:o + 12]))
        sum1, mean, varr, rstd, tmp1 = (slots[0][k] for k in
                                        ("sum1", "mean", "varr", "rstd", "tmp1"))
        sqp = slots[0]["sqp"]
        dtmp8 = sml[:, 32:40]
        rden8 = sml[:, 40:48]

        ps_q = [psum("ps_q0", [128, 1024], F32), psum("ps_q1", [128, 1024], F32)]
        ps_r = psum("ps_r", [128, 512], F32)
        ps_b = psum("ps_b", [128, 512], F32)
        ps_pair = [psum("ps_f", [128, 512], F32), psum("ps_g", [128, 512], F32)]
        bank_f, bank_g = ("f", ps_pair[0]), ("g", ps_pair[1])
        bank_r, bank_b = ("r", ps_r), ("b", ps_b)

        rot = {"i": 0, "last": {}, "cycle": None, "ei": 0}

        def set_rot(named_banks, seeds=None):
            rot["i"] = 0
            rot["cycle"] = named_banks
            if seeds:
                for name, op in seeds.items():
                    rot["last"][name] = op

        def tx_group(src_tiles, dst_ap, eng_ev, deps_tx=(), deps_ev=(),
                     ident=None, accum=None, scale_cols=None, tag=""):
            cyc = rot["cycle"]
            bname, pp = cyc[rot["i"] % len(cyc)]
            rot["i"] += 1
            if eng_ev == "alt":
                eng_ev = "scalar" if rot["ei"] % 2 == 0 else "vector"
                rot["ei"] += 1
            idm = ident if ident is not None else id_f32

            def tx_fn(e, pp=pp):
                out = None
                for k, tile in enumerate(src_tiles):
                    if accum:
                        for rr, t in enumerate(tile):
                            out = e.matmul(pp[:, k * 128:(k + 1) * 128], t, idm,
                                           is_transpose=True, start=(rr == 0),
                                           stop=(rr == len(tile) - 1))
                    else:
                        out = e.matmul(pp[:, k * 128:(k + 1) * 128], tile, idm,
                                       is_transpose=True, start=True, stop=True)
                return out
            otx = P.add("tensor", tx_fn,
                        deps=list(deps_tx) + [rot["last"].get(bname)], tag=f"{tag}_tx")
            n = len(src_tiles)
            if scale_cols is None:
                def ev_fn(e, pp=pp, n=n, eng_ev=eng_ev):
                    if eng_ev == "scalar":
                        return e.copy(dst_ap, pp[:, 0:n * 128])
                    return e.tensor_copy(dst_ap, pp[:, 0:n * 128])
            else:
                def ev_fn(e, pp=pp, eng_ev=eng_ev):
                    last = None
                    for k, (dcol, scol) in enumerate(zip(dst_ap, scale_cols)):
                        src = pp[:, k * 128:(k + 1) * 128]
                        if eng_ev == "vector":
                            last = e.tensor_scalar(dcol, src, scol, None, OP.mult)
                        else:
                            last = e.activation(dcol, src, AF.Copy, scale=scol)
                    return last
            oev = P.add(eng_ev, ev_fn, deps=[otx] + list(deps_ev), tag=f"{tag}_ev")
            rot["last"][bname] = oev
            return otx, oev

        o_bq = P.add("sync", lambda e: e.dma_start(bq_sb[0:1, :], bq_d[None, :]),
                     dma=True, key="bq", tag="bq")
        o_z = P.add("sync", lambda e: e.dma_start(znat[:, :], z_d[:, :]),
                    dma=True, key="z", tag="z")
        o_m = P.add("sync", lambda e: e.dma_start(
            m_st.rearrange("p (h e) -> p h e", e=DH),
            m_d[:, :, :].transpose([1, 0, 2])), dma=True, key="m", tag="m")
        o_sc = P.add("sync", lambda e: e.dma_start(
            sc_row[0:1, :].rearrange("p (a b) -> p a b", b=32),
            sc_d[None, :, :]), dma=True, key="sc", tag="sc")

        o_io1 = P.add("gpsimd", lambda e: e.iota(
            irow, [[1, 128]], base=0, channel_multiplier=0,
            allow_small_or_imprecise_dtypes=True), tag="iota_row")
        o_io2 = P.add("gpsimd", lambda e: e.iota(
            ipart[:, 0:1], [[1, 1]], base=0, channel_multiplier=1,
            allow_small_or_imprecise_dtypes=True), tag="iota_part")
        o_id = P.add("vector", lambda e: e.tensor_scalar(
            id_f32, irow, ipart[:, 0:1], None, OP.is_equal),
            deps=[o_io1, o_io2], tag="id_f32")
        o_idb = P.add("vector", lambda e: e.tensor_copy(id_bf, id_f32),
                      deps=[o_id], tag="id_bf")
        o_ones = P.add("gpsimd", lambda e: e.memset(ones_bf[0:1, :], 1.0), tag="ones")
        o_onesf = P.add("gpsimd", lambda e: e.memset(ones_f32[0:1, :], 1.0), tag="ones_f32")
        o_eps = P.add("gpsimd", lambda e: e.memset(eps_sb[:, 0:1], 1e-5), tag="eps")
        o_bqb = P.add("vector", lambda e: e.tensor_copy(bq_bf[0:1, :], bq_sb[0:1, :]),
                      deps=[o_bq], tag="bq_bf")
        o_zt = P.add("tensor", lambda e: e.transpose(ps_r[:, 0:H], znat[0:H, :],
                                                     id_f32[0:H, 0:H]),
                     deps=[o_z, o_id], tag="z_T")
        o_zc = P.add("vector", lambda e: e.tensor_copy(zcol[:, :], ps_r[:, 0:H]),
                     deps=[o_zt], tag="zcol")
        o_ma = P.add("vector", lambda e: e.tensor_copy(
            M_aug.rearrange("p (h c) -> p h c", c=129)[:, :, 0:DH],
            m_st.rearrange("p (h e) -> p h e", e=DH)), deps=[o_m], tag="M_aug")
        o_mz = P.add("vector", lambda e: e.tensor_copy(
            M_aug.rearrange("p (h c) -> p h c", c=129)[:, :, 128:129],
            zcol.rearrange("p (h o) -> p h o", o=1)), deps=[o_zc, o_ma], tag="M_aug_z")
        o_scmm = P.add("tensor", lambda e: [
            e.matmul(ps_q[0][:, hh * 512:(hh + 1) * 512], ones_f32[0:1, 0:128],
                     sc_row[0:1, hh * 512:(hh + 1) * 512], start=True, stop=True)
            for hh in range(2)],
            deps=[o_sc, o_onesf], tag="sc_mm")
        o_scb = P.add("vector", lambda e: e.tensor_copy(sc_bc[:, :], ps_q[0][:, 0:1024]),
                      deps=[o_scmm], tag="sc_bc")

        set_rot([bank_f, bank_g])
        slot = [pool[:, k * 2048:(k + 1) * 2048] for k in range(6)]
        slot_last = [None] * 6
        _wcnt = [0]

        def w_engine():
            n = _wcnt[0]
            _wcnt[0] += 1
            if n >= 54:
                return "gpsimd", f"gw{n}"
            return ("sync", "scalar")[n % 2], None

        def w_load(sl, srcap, deps, tag):
            weng, wkey = w_engine()
            extra = []
            if slot_last[sl] is None:
                if sl == 0:
                    extra = [o_id]
                elif sl == 4:
                    extra = [o_ma, o_scb]
                elif sl == 5:
                    extra = [o_bqb]
            od = P.add(weng, lambda e, sl=sl, srcap=srcap: e.dma_start(
                slot[sl], srcap),
                deps=[slot_last[sl]] + extra + list(deps),
                dma=True, key=wkey or f"w{sl}", tag=tag)
            return od

        wq_all = []
        for g in range(16):
            wq_all.append(P.add("gpsimd", lambda e, g=g: e.dma_start(
                WqT[:, g * 2048:(g + 1) * 2048].rearrange(
                    "p (l c) -> p l c", l=2),
                wq_d[2 * g:2 * g + 2].transpose([1, 0, 2])),
                dma=True, key=f"wqg{g}", tag=f"wq_dma{g}"))
        wq_ready = P.add("gpsimd", lambda e: e.memset(ipart[:, 0:1], 1.0),
                         deps=wq_all + [o_id], tag="wq_bar")
        wq_ready2 = None
        xt0_op = P.add("gpsimd", lambda e: e.dma_start(
            x_fm.rearrange("p (it r) -> p it r", r=128), xt_d[0]),
            dma=True, key="xt0", tag="A_xfm0")

        add3s = [None, None]
        for fold in range(16):
            ht, hf = fold // 2, fold % 2
            st3 = fold % 2
            a, b, c = (0, 1, 2) if st3 == 0 else (3, 4, 5)
            cols = slice(hf * 2048, (hf + 1) * 2048)
            lds = []
            for r, sl in ((0, a), (1, b), (2, c)):
                lds.append(w_load(sl, wo_d[ht, r][:, cols], [], f"wo{fold}_{r}"))
            oa1 = P.add("vector", lambda e, a=a, b=b: e.tensor_add(
                slot[a], slot[a], slot[b]),
                deps=[lds[0], lds[1]], tag=f"wo_a1_{fold}")
            slot_last[b] = oa1
            ld3 = w_load(b, wo_d[ht, 3][:, cols], [], f"wo{fold}_3")
            oa2 = P.add("vector", lambda e, a=a, c=c: e.tensor_add(
                slot[a], slot[a], slot[c]),
                deps=[lds[2], oa1], tag=f"wo_a2_{fold}")
            slot_last[c] = oa2
            add3 = P.add("vector", lambda e, ht=ht, hf=hf, a=a, b=b: e.tensor_add(
                WoT[:, ht * D + hf * 2048: ht * D + (hf + 1) * 2048],
                slot[a], slot[b]),
                deps=[oa2, ld3], tag=f"wo_a3_{fold}")
            slot_last[a] = add3
            slot_last[b] = add3
            add3s[st3] = add3
        wo_ready = add3s[0]
        wo_ready2 = add3s[1]

        a_dma = [None] * NS
        a_q = [None] * NS
        a_sig = [None] * NS
        a_txs = [None] * NS
        a_txr = [None] * NS
        a_t4 = [None] * NS
        a_spl = [[None, None] for _ in range(NS)]
        a_res_last = [None] * NS
        a_var = [None] * NS
        a_nrm_v = [None] * NS
        a_nrm_s = [None] * NS
        a_osq = [None] * NS
        a_dmul_last = [None] * NS
        a_resr_last = [None] * NS
        a_xev = [None] * NS
        last_psb = [None]
        last_psr = [o_zc]

        def ln_chain(tagp, sq_ops, osum, onr_prev, osq_prev, sl):
            base = [o for o in onr_prev if o is not None]
            om = P.add("vector", lambda e: e.tensor_scalar(
                sl["mean"], sl["sum1"], 1.0 / D, None, OP.mult),
                deps=[osum] + base, tag=f"{tagp}_mean")
            or2 = P.add("vector", lambda e: e.reduce_sum(sl["tmp1"], sl["sqp"], AXX),
                        deps=sq_ops + base, tag=f"{tagp}_sqsum")
            ov1 = P.add("vector", lambda e: e.tensor_scalar(
                sl["varr"], sl["tmp1"], 1.0 / D, None, OP.mult),
                deps=[or2, osq_prev], tag=f"{tagp}_var1")
            ov2 = P.add("vector", lambda e: e.tensor_tensor(
                sl["tmp1"], sl["mean"], sl["mean"], OP.mult),
                deps=[om, ov1], tag=f"{tagp}_msq")
            ov3 = P.add("vector", lambda e: e.tensor_sub(sl["varr"], sl["varr"], sl["tmp1"]),
                        deps=[ov1, ov2], tag=f"{tagp}_var")
            return ov3

        def emit_xdma(s):
            if s >= NS:
                return
            a_dma[s] = P.add("scalar", lambda e, s=s: e.dma_start(
                xbuf[s % 2], x_d[s * 128:(s + 1) * 128, :]),
                deps=[a_t4[s - 2] if s >= 2 else None,
                      wo_ready if s <= 1 else None,
                      wo_ready2 if s <= 1 else None],
                dma=True, key=f"x{s%2}", tag=f"A_xdma{s}")

        def emit_xtx(s):
            if s == 0:
                a_xev[0] = [xt0_op]
                return
            a_xev[s] = [P.add("gpsimd", lambda e, s=s: e.dma_start(
                x_fm.rearrange("p (it r) -> p it r", r=128),
                xt_d[s]),
                deps=[a_q[s - 1]],
                dma=True, key=f"xt{s}", tag=f"A_xfm{s}")]

        def emit_q(s):
            def q_fn(e, s=s):
                last = None
                for half in range(2):
                    dst = ps_q[s % 2][:, half * 512:(half + 1) * 512]
                    for it in range(NT):
                        e.matmul(dst, x_fm[:, it * 128:(it + 1) * 128],
                                 WqT[:, it * 1024 + half * 512: it * 1024 + half * 512 + 512],
                                 start=(it == 0), stop=False)
                    last = e.matmul(dst, ones_bf[0:1, 0:128],
                                    bq_bf[0:1, half * 512:(half + 1) * 512],
                                    start=False, stop=True)
                return last
            a_q[s] = P.add("tensor", q_fn,
                           deps=a_xev[s] + [o_ones, o_bqb, wq_ready, wq_ready2,
                                            o_scb if s == 0 else None,
                                            a_dmul_last[s - 1] if s >= 1 else None],
                           tag=f"A_q{s}")

        def backQ1(s):
            oexp = P.add("scalar", lambda e, s=s: e.activation(
                t0, ps_q[s % 2][:, :], AF.Exp),
                deps=[a_q[s],
                      a_spl[s - 1][0] if s >= 1 else wo_ready,
                      None if s >= 1 else wo_ready2],
                tag=f"A_exp{s}")
            orel = P.add("vector", lambda e, s=s: e.tensor_scalar(
                retr_f32, ps_q[s % 2][:, :], 0.0, None, OP.max),
                deps=[a_q[s], oexp,
                      a_txr[s - 1] if s >= 1 else None,
                      a_spl[s - 1][0] if s >= 1 else wo_ready,
                      None if s >= 1 else wo_ready2],
                tag=f"A_rel{s}")
            a_sig[s] = P.add("vector", lambda e: e.scalar_tensor_tensor(
                t0, t0, 1.0, retr_f32, OP.min, OP.add),
                deps=[oexp, orel], tag=f"A_sig{s}")
            sev = []
            for g in range(2):
                tiles = [t0[:, (g * 4 + k) * 128:(g * 4 + k + 1) * 128]
                         for k in range(4)]
                otx, oev = tx_group(tiles, sig_fm[:, g * 512:(g + 1) * 512], "alt",
                                    deps_tx=[a_sig[s]],
                                    tag=f"A_st{s}_{g}")
                sev.append(oev)
                a_txs[s] = otx
            psq_o = ps_q[(s + 1) % 2]
            rslot = [(psq_o, 0), (psq_o, 129), (psq_o, 258),
                     (psq_o, 512), (psq_o, 641), (psq_o, 770),
                     (ps_r, 0), (ps_r, 129)]
            omm = P.add("tensor", lambda e, rslot=rslot: [
                e.matmul(rslot[h][0][:, rslot[h][1]:rslot[h][1] + 129],
                         sig_fm[:, h * 128:(h + 1) * 128],
                         M_aug[:, h * 129:(h + 1) * 129],
                         start=True, stop=True)
                for h in range(8)],
                deps=[sev[0], sev[1], o_mz, last_psr[0],
                      a_sig[s - 1] if s >= 1 else None,
                      a_res_last[s - 1] if s >= 1 else None,
                      a_resr_last[s - 1] if s >= 1 else None,
                      o_scb if s == 0 else None],
                tag=f"A_ret{s}")
            den_aps = [
                (psq_o[:, 128:515].rearrange("p (e c) -> p e c", c=129)[:, :, 0:1], 0, 3),
                (psq_o[:, 640:898].rearrange("p (e c) -> p e c", c=129)[:, :, 0:1], 3, 2),
                (psq_o[:, 898:899].rearrange("p (e c) -> p e c", c=1), 5, 1),
                (ps_r[:, 128:386].rearrange("p (e c) -> p e c", c=129)[:, :, 0:1], 6, 2),
            ]
            dd = []
            for k, (dn, off, cnt) in enumerate(den_aps):
                dd.append(P.add("vector", lambda e, dn=dn, off=off, cnt=cnt: e.tensor_scalar(
                    dtmp8[:, off:off + cnt].rearrange("p (e c) -> p e c", c=1),
                    dn, 1e-6, None, OP.add),
                    deps=[omm,
                          a_dmul_last[s - 1] if (s >= 1 and k == 0) else None],
                    tag=f"A_dden{s}_{k}"))
            orcp = P.add("vector", lambda e: e.reciprocal(rden8, dtmp8),
                         deps=dd, tag=f"A_drcp{s}")
            omul = None
            for h in range(8):
                omul = P.add("vector", lambda e, h=h, rslot=rslot: e.tensor_scalar(
                    retr_f32[:, h * 128:(h + 1) * 128],
                    rslot[h][0][:, rslot[h][1]:rslot[h][1] + 128],
                    rden8[:, h:h + 1], None, OP.mult),
                    deps=[orcp, a_sig[s] if h == 0 else None],
                    tag=f"A_dmul{s}_{h}")
            a_dmul_last[s] = omul
            last_psr[0] = omul

        def backQ2(s):
            prev_div = a_dmul_last[s]
            rev = []
            for g in range(2):
                tiles = [retr_f32[:, (g * 4 + k) * 128:(g * 4 + k + 1) * 128]
                         for k in range(4)]
                otx, oev = tx_group(tiles, retr_fm[:, g * 512:(g + 1) * 512], "alt",
                                    deps_tx=[prev_div],
                                    tag=f"A_rt{s}_{g}")
                rev.append(oev)
                a_txr[s] = otx
            res_ops = []
            for oc in range(8):
                pdst = (ps_b, ps_q[s % 2][:, 0:512], ps_r)[oc % 3]
                first_dep = (last_psb[0] if oc == 0 else
                             a_sig[s] if oc == 1 else
                             prev_div)
                omm = P.add("tensor", lambda e, oc=oc, pdst=pdst: [
                    e.matmul(pdst, retr_fm[:, ht * 128:(ht + 1) * 128],
                             WoT[:, ht * D + oc * 512: ht * D + oc * 512 + 512],
                             start=(ht == 0), stop=(ht == 7))
                    for ht in range(8)],
                    deps=[rev[0], rev[1], wo_ready, wo_ready2,
                          res_ops[oc - 3] if oc >= 3 else first_dep],
                    tag=f"A_att{s}_{oc}")
                ores = P.add("vector", lambda e, s=s, oc=oc, pdst=pdst: e.tensor_add(
                    xbuf[s % 2][:, oc * 512:(oc + 1) * 512], pdst,
                    xbuf[s % 2][:, oc * 512:(oc + 1) * 512]),
                    deps=[omm, a_dma[s] if oc == 0 else None],
                    tag=f"A_res{s}_{oc}")
                res_ops.append(ores)
            last_psb[0] = res_ops[6]
            a_resr_last[s] = res_ops[5]
            a_res_last[s] = res_ops[-1]
            osum = P.add("vector", lambda e, s=s: e.reduce_sum(sum1, xbuf[s % 2][:, :], AXX),
                         deps=[res_ops[-1], a_nrm_v[s - 1] if s >= 1 else None,
                               a_nrm_s[s - 1] if s >= 1 else None],
                         tag=f"A_sum{s}")
            sq_ops = []
            for g in range(4):
                sq_ops.append(P.add("scalar", lambda e, s=s, g=g: e.activation(
                    sqjunk, xbuf[s % 2][:, g * 1024:(g + 1) * 1024], AF.Square,
                    accum_out=sqp[:, g:g + 1]),
                    deps=[res_ops[-1], a_var[s - 1] if s >= 1 else None,
                          sq_ops[-1] if g > 0 else None,
                          (a_spl[s - 1][1] if s >= 1 else wo_ready) if g == 0 else None,
                          (None if s >= 1 else wo_ready2) if g == 0 else None],
                    tag=f"A_sq{s}_{g}"))
            a_var[s] = ln_chain(f"A{s}", sq_ops, osum,
                                [a_nrm_v[s - 1], a_nrm_s[s - 1]] if s >= 1 else [],
                                a_osq[s - 1] if s >= 1 else None, slots[0])
            osq = P.add("scalar", lambda e: e.activation(tmp1, varr, AF.Sqrt,
                                                         bias=eps_sb[:, 0:1]),
                        deps=[a_var[s], o_eps], tag=f"A_sqrt{s}")
            a_osq[s] = osq
            orc = P.add("vector", lambda e: e.reciprocal(rstd, tmp1),
                        deps=[osq, a_nrm_v[s - 1] if s >= 1 else None,
                              a_nrm_s[s - 1] if s >= 1 else None], tag=f"A_rcp{s}")
            omrs = P.add("vector", lambda e: e.scalar_tensor_tensor(
                slots[0]["mrs"], mean, -1.0, rstd, OP.mult, OP.mult),
                deps=[orc, a_nrm_s[s - 1] if s >= 1 else None], tag=f"A_mrs{s}")
            a_nrm_v[s] = P.add("vector", lambda e, s=s: e.tensor_scalar(
                xbuf[s % 2][:, 0:2048], xbuf[s % 2][:, 0:2048],
                mean[:, 0:1], rstd[:, 0:1],
                OP.subtract, OP.mult),
                deps=[orc], tag=f"A_nrmv{s}")
            a_nrm_s[s] = P.add("scalar", lambda e, s=s: e.activation(
                xbuf[s % 2][:, 2048:4096], xbuf[s % 2][:, 2048:4096], AF.Identity,
                bias=slots[0]["mrs"][:, 0:1], scale=rstd[:, 0:1]),
                deps=[omrs], tag=f"A_nrms{s}")

        def t4part(s):
            t4e = {}
            for g in range(8):
                tiles = [xbuf[s % 2][:, (g * 4 + k) * 128:(g * 4 + k + 1) * 128]
                         for k in range(4)]
                otx, oev = tx_group(
                    tiles, stg[g // 4][:, (g % 4) * 512:((g % 4) + 1) * 512], "alt",
                    deps_tx=[a_nrm_v[s] if g < 4 else a_nrm_s[s]],
                    deps_ev=[a_spl[s - 1][g // 4] if s >= 1 else None],
                    tag=f"A_t4_{s}_{g}")
                t4e[g] = oev
                if g == 7:
                    a_t4[s] = otx
            for hf in range(2):
                a_spl[s][hf] = P.add("scalar", lambda e, s=s, hf=hf: e.dma_start(
                    scr1[s, :, hf * 16:(hf + 1) * 16, :],
                    stg[hf].rearrange("p (it r) -> p it r", r=128)),
                    deps=[t4e[4 * hf + gg] for gg in range(4)],
                    dma=True, key=f"spl{hf}", tag=f"A_spl{s}_{hf}")

        emit_xdma(0)
        emit_xdma(1)
        emit_xtx(0)
        emit_q(0)
        backQ1(0)
        for s in range(1, NS):
            emit_xtx(s)
            backQ2(s - 1)
            emit_q(s)
            t4part(s - 1)
            emit_xdma(s + 1)
            backQ1(s)
        backQ2(NS - 1)
        t4part(NS - 1)

        a_all_sp0 = a_spl[NS - 1][0]
        a_all_sp1 = a_spl[NS - 1][1]
        p2_first_deps = [a_q[NS - 1], a_res_last[NS - 1], a_sig[NS - 1]]

        owd0a = P.add("sync", lambda e: e.dma_start(
            wf0st[:, 0:2048], wf_d[0][:, 0:2048]),
            deps=[a_all_sp0, a_all_sp1], dma=True, key="wf", tag="B_wdma0a")
        owd0b = P.add("scalar", lambda e: e.dma_start(
            wf0st[:, 2048:4096], wf_d[0][:, 2048:4096]),
            deps=[a_all_sp0, a_all_sp1], dma=True, key="wf0b", tag="B_wdma0b")
        evs0 = []
        for g in range(8):
            eng_pick = "vector" if g % 2 == 0 else "scalar"

            def dq0_fn(e, g=g, eng_pick=eng_pick):
                last = None
                for k in range(4):
                    it = g * 4 + k
                    dst = wft[0][:, it * 128:(it + 1) * 128]
                    srcc = wf0st[:, it * 128:(it + 1) * 128]
                    scal = sc_bc[:, it:it + 1]
                    if eng_pick == "vector":
                        last = e.tensor_scalar(dst, srcc, scal, None, OP.mult)
                    else:
                        last = e.activation(dst, srcc, AF.Copy, scale=scal)
                return last
            evs0.append(P.add(eng_pick, dq0_fn,
                              deps=[owd0a if g < 4 else owd0b, o_scb],
                              tag=f"B_dq0_{g}"))

        nsh = 3 if NS >= 3 else 1
        shsz = (NS + nsh - 1) // nsh
        pair_dst = [pool[:, 0:4096], pool[:, 4096:8192]]
        pair_p2 = {}
        p2_ops = []
        for pr in range(NT // 2):
            it0 = 2 * pr
            dst = pair_dst[pr % 2]
            halves = []
            for sh in range(nsh):
                lo, hi = sh * shsz, min(NS, (sh + 1) * shsz)
                if lo >= hi:
                    continue
                eng3 = ("sync", "scalar", "gpsimd")[sh % 3]
                key3 = f"p1g{pr}" if eng3 == "gpsimd" else f"p1_{pr % 2}_{sh}"
                od = P.add(eng3,
                           lambda e, it0=it0, lo=lo, hi=hi, dst=dst: e.dma_start(
                    dst.rearrange("p (s l r) -> p s l r", l=2, r=128)[:, lo:hi],
                    scr1[:, :, it0:it0 + 2, :].transpose([1, 0, 2, 3])[:, lo:hi]),
                    deps=[a_all_sp0, a_all_sp1,
                          pair_p2.get(pr - 2),
                          a_t4[NS - 2] if (pr <= 1 and NS >= 2) else None,
                          a_t4[NS - 1] if pr <= 1 else None],
                    dma=True, key=key3, tag=f"P1_{pr}_{sh}")
                halves.append(od)
            lastp = None
            for itl in range(2):
                lastp = P.add("vector",
                              lambda e, it0=it0, itl=itl, dst=dst: e.tensor_copy(
                    x1fm[:, (it0 + itl) * R:(it0 + itl + 1) * R],
                    dst.rearrange("p (s l r) -> p s l r", l=2, r=128)[:, 0:NS, itl, :]),
                    deps=halves + (p2_first_deps if pr == 0 else []),
                    tag=f"P2_{it0 + itl}")
                p2_ops.append(lastp)
            pair_p2[pr] = lastp
        x1fm_ready = pair_p2[NT // 2 - 1]
        p2_last = [pair_p2[NT // 2 - 1], pair_p2[NT // 2 - 2]]

        set_rot([bank_f, bank_g, bank_r, bank_b],
                seeds={"r": a_resr_last[NS - 1], "b": last_psb[0]})
        NN = max(1, R // 512)
        nw = min(512, R)
        btx_last = {}
        btx2_last = {}
        b_stat = [None]
        bjunk = pbf[:, 8192:9216]
        b_mm = [None] * NT
        b_sf = [None] * NT
        b_t6 = [None] * NT
        b_spl = [None] * NT

        def emit_bt6(oc):
            ngr = (NS + 3) // 4
            ev6 = []
            for g in range(ngr):
                kk = min(4, NS - g * 4)
                tiles = [sfm[oc % 2][:, (g * 4 + k) * 128:(g * 4 + k + 1) * 128]
                         for k in range(kk)]
                otx, oev = tx_group(
                    tiles, srmst[:, g * 512:g * 512 + kk * 128], "scalar",
                    deps_tx=[b_sf[oc]],
                    deps_ev=[b_spl[oc - 1] if (oc >= 1 and g == 0) else None,
                             b_stat[0] if (oc >= 1 and g == 0) else None],
                    tag=f"B_s6{oc}_{g}")
                if g == ngr - 1:
                    b_t6[oc] = otx
                ev6.append(oev)
            b_spl[oc] = P.add("scalar", lambda e, oc=oc: e.dma_start(
                scr2[:, :, oc, :].transpose([1, 0, 2]),
                srmst[:, 0:NS * 128].rearrange("p (s o) -> p s o", o=128)),
                deps=ev6, dma=True, key="bspl", tag=f"B_spl{oc}")
            sr3 = srmst[:, 0:NS * 128].rearrange("p (s o) -> p s o", o=128)
            o_rs = P.add("vector", lambda e: e.reduce_sum(
                acc_tmp[:, 0:NS], sr3, AXX), deps=ev6 + [b_stat[0]], tag=f"B_rs{oc}")
            if oc == 0:
                o_as = P.add("vector", lambda e: e.tensor_copy(
                    acc_sum[:, 0:NS], acc_tmp[:, 0:NS]),
                    deps=[o_rs, a_nrm_v[NS - 1], a_nrm_s[NS - 1]], tag=f"B_as{oc}")
            else:
                o_as = P.add("vector", lambda e: e.tensor_add(
                    acc_sum[:, 0:NS], acc_sum[:, 0:NS], acc_tmp[:, 0:NS]),
                    deps=[o_rs], tag=f"B_as{oc}")
            prev = o_as
            half = (NS + 1) // 2
            for hh in range(2):
                lo, hi = hh * half, min(NS, (hh + 1) * half)
                if lo >= hi:
                    continue
                osq = P.add("vector", lambda e, lo=lo, hi=hi: e.tensor_tensor(
                    bjunk[:, 0:(hi - lo) * 128],
                    srmst[:, lo * 128:hi * 128], srmst[:, lo * 128:hi * 128],
                    OP.mult), deps=[prev], tag=f"B_sq{oc}_{hh}")
                orr = P.add("vector", lambda e, lo=lo, hi=hi: e.reduce_sum(
                    acc_tmp[:, lo:hi],
                    bjunk[:, 0:(hi - lo) * 128].rearrange("p (s o) -> p s o", o=128),
                    AXX), deps=[osq], tag=f"B_rq{oc}_{hh}")
                if oc == 0:
                    prev = P.add("vector", lambda e, lo=lo, hi=hi: e.tensor_copy(
                        acc_sq[:, lo:hi], acc_tmp[:, lo:hi]),
                        deps=[orr], tag=f"B_aq{oc}_{hh}")
                else:
                    prev = P.add("vector", lambda e, lo=lo, hi=hi: e.tensor_add(
                        acc_sq[:, lo:hi], acc_sq[:, lo:hi], acc_tmp[:, lo:hi]),
                        deps=[orr], tag=f"B_aq{oc}_{hh}")
            b_stat[0] = prev

        for oc in range(NT):
            if oc == 0:
                evs = evs0
            else:
                owd = P.add("sync", lambda e, oc=oc: e.dma_start(
                    wfst, wf_d[oc]),
                    deps=[btx_last[oc - 1], btx2_last[oc - 1]] +
                         ([pair_p2[NT // 2 - 2]] if oc == 1 else []),
                    dma=True, key="wf", tag=f"B_wdma{oc}")
                evs = []
                for g in range(8):
                    eng_pick = "vector" if g % 2 == 0 else "scalar"

                    def dq_fn(e, oc=oc, g=g, eng_pick=eng_pick):
                        last = None
                        for k in range(4):
                            it = g * 4 + k
                            dst = wft[oc % 2][:, it * 128:(it + 1) * 128]
                            srcc = wfst[:, it * 128:(it + 1) * 128]
                            scal = sc_bc[:, oc * 32 + it:oc * 32 + it + 1]
                            if eng_pick == "vector":
                                last = e.tensor_scalar(dst, srcc, scal, None, OP.mult)
                            else:
                                last = e.activation(dst, srcc, AF.Copy, scale=scal)
                        return last
                    oev = P.add(eng_pick, dq_fn,
                                deps=[owd, o_scb, b_mm[oc - 2] if oc >= 2 else None],
                                tag=f"B_dq{oc}_{g}")
                    evs.append(oev)
            btx_last[oc] = evs[-1]
            btx2_last[oc] = evs[-2]

            if oc >= 1:
                emit_bt6(oc - 1)

            def mm_fn(e, oc=oc, itlist=None):
                last = None
                for it in itlist:
                    for nn in range(NN):
                        t = ps_q[0] if nn < 2 else ps_q[1]
                        dst = t[:, (nn % 2) * nw:(nn % 2) * nw + nw]
                        last = e.matmul(dst, wft[oc % 2][:, it * 128:(it + 1) * 128],
                                        x1fm[:, it * R + nn * nw: it * R + nn * nw + nw],
                                        start=(it == 0), stop=(it == NT - 1))
                return last
            base_deps = [b_sf[oc - 1] if oc >= 1 else None,
                         a_dmul_last[NS - 1] if oc == 0 else None,
                         a_res_last[NS - 1] if oc == 0 else None]
            if oc <= 1:
                last_mm = None
                for kq in range(4):
                    itl = list(range(kq * 8, (kq + 1) * 8))
                    last_mm = P.add(
                        "tensor",
                        lambda e, oc=oc, itl=itl: mm_fn(e, oc=oc, itlist=itl),
                        deps=evs + base_deps + [p2_ops[itl[-1]], last_mm],
                        tag=f"B_mm{oc}_{kq}")
                b_mm[oc] = last_mm
            else:
                b_mm[oc] = P.add("tensor",
                                 lambda e, oc=oc: mm_fn(e, oc=oc, itlist=list(range(NT))),
                                 deps=evs + base_deps, tag=f"B_mm{oc}")
            ord_ = P.add("scalar", lambda e, oc=oc: e.dma_start(
                x1res[:, 0:NS * 128].rearrange("p (s r) -> p s r", r=128),
                scr1[:, :, oc, :].transpose([1, 0, 2])),
                deps=[a_all_sp0, a_all_sp1, b_sf[oc - 1] if oc >= 1 else None,
                      pair_p2[NT // 2 - 1] if oc == 0 else None],
                dma=True, key="bres", tag=f"B_res{oc}")

            def sf_fn(e, oc=oc):
                if R >= 2048:
                    e.tensor_add(sfm[oc % 2][:, 0:1024], ps_q[0][:, :], x1res[:, 0:1024])
                    return e.tensor_add(sfm[oc % 2][:, 1024:2048], ps_q[1][:, :],
                                        x1res[:, 1024:2048])
                return e.tensor_add(sfm[oc % 2][:, 0:R], ps_q[0][:, 0:R], x1res[:, 0:R])
            b_sf[oc] = P.add("vector", sf_fn,
                             deps=[b_mm[oc], ord_, b_t6[oc - 2] if oc >= 2 else None],
                             tag=f"B_sf{oc}")
        emit_bt6(NT - 1)

        c_out = []
        c_nrm = [a_nrm_s[NS - 1], None]
        c_osq = [a_osq[NS - 1], None]
        stat_fin = b_stat[0]
        for s in range(NS):
            par = s % 2
            sl = slots[par]
            od = P.add("sync" if par == 0 else "scalar",
                       lambda e, s=s: e.dma_start(srm[s % 2], scr2[s]),
                       deps=[b_spl[NT - 1], stat_fin,
                             c_out[s - 2] if s >= 2 else None],
                       dma=True, key=f"cin{s%2}", tag=f"C_dma{s}")
            om = P.add("vector", lambda e, s=s, sl=sl: e.tensor_scalar(
                sl["mean"], acc_sum[:, s:s + 1], 1.0 / D, None, OP.mult),
                deps=[stat_fin, c_nrm[par]], tag=f"C_mean{s}")
            ov1 = P.add("vector", lambda e, s=s, sl=sl: e.tensor_scalar(
                sl["varr"], acc_sq[:, s:s + 1], 1.0 / D, None, OP.mult),
                deps=[stat_fin, c_osq[par]], tag=f"C_var1{s}")
            ov2 = P.add("vector", lambda e, sl=sl: e.tensor_tensor(
                sl["tmp1"], sl["mean"], sl["mean"], OP.mult),
                deps=[om, ov1], tag=f"C_msq{s}")
            ov3 = P.add("vector", lambda e, sl=sl: e.tensor_sub(
                sl["varr"], sl["varr"], sl["tmp1"]),
                deps=[ov1, ov2], tag=f"C_var{s}")
            osq = P.add("scalar", lambda e, sl=sl: e.activation(
                sl["tmp1"], sl["varr"], AF.Sqrt, bias=eps_sb[:, 0:1]),
                deps=[ov3, o_eps], tag=f"C_sqrt{s}")
            c_osq[par] = osq
            orc = P.add("vector", lambda e, sl=sl: e.reciprocal(sl["rstd"], sl["tmp1"]),
                        deps=[osq, c_nrm[par]],
                        tag=f"C_rcp{s}")
            if par == 0:
                onr = P.add("vector", lambda e, s=s, sl=sl: e.tensor_scalar(
                    srm[s % 2], srm[s % 2], sl["mean"][:, 0:1], sl["rstd"][:, 0:1],
                    OP.subtract, OP.mult),
                    deps=[orc, od], tag=f"C_nrm{s}")
            else:
                omrs = P.add("vector", lambda e, sl=sl: e.scalar_tensor_tensor(
                    sl["mrs"], sl["mean"], -1.0, sl["rstd"], OP.mult, OP.mult),
                    deps=[orc], tag=f"C_mrs{s}")
                onr = P.add("scalar", lambda e, s=s, sl=sl: e.activation(
                    srm[s % 2], srm[s % 2], AF.Identity,
                    bias=sl["mrs"][:, 0:1], scale=sl["rstd"][:, 0:1]),
                    deps=[omrs, od], tag=f"C_nrm{s}")
            c_nrm[par] = onr
            if par == 0:
                oeng, okey = "gpsimd", f"cog{s}"
            else:
                oeng, okey = ("sync", "couts") if s % 4 == 1 else ("scalar", "coutc")
            c_out.append(P.add(oeng, lambda e, s=s: e.dma_start(
                y_d[s * 128:(s + 1) * 128, :], srm[s % 2]),
                deps=[onr], dma=True, key=okey, tag=f"C_out{s}"))

        P.add("vector", lambda e: e.engine_nop(), deps=list(c_out), tag="final_wait")

        P.finalize()
        sems = {}
        for en in ("sync", "scalar", "vector", "tensor", "gpsimd"):
            sems[en] = ctx.enter_context(nc.semaphore(f"sem_{en}"))
        for k in P.keys:
            sems["dma_" + k] = ctx.enter_context(nc.semaphore(f"sem_d_{k}"))
        with nc.Block() as block:
            @block.sync
            def _(eng):
                P.emit("sync", eng, sems)

            @block.scalar
            def _(eng):
                P.emit("scalar", eng, sems)

            @block.vector
            def _(eng):
                P.emit("vector", eng, sems)

            @block.tensor
            def _(eng):
                P.emit("tensor", eng, sems)

            @block.gpsimd
            def _(eng):
                P.emit("gpsimd", eng, sems)

    return nc


def kernel(**inputs):
    x = np.asarray(inputs["L0_attn_hidden_states"], dtype=np.float32)
    Wq = np.asarray(inputs["L0_attn_Wq"], dtype=np.float32)
    bq = np.asarray(inputs["L0_attn_bq"], dtype=np.float32)
    Wo = np.asarray(inputs["L0_attn_Wo"], dtype=np.float32)
    M = np.asarray(inputs["L0_attn_M"], dtype=np.float32)
    z = np.asarray(inputs["L0_attn_z"], dtype=np.float32)
    Wf = np.asarray(inputs["L0_ffn_weight"], dtype=np.float32)
    sc = np.asarray(inputs["L0_ffn_weight_scale"], dtype=np.float32)

    S = x.shape[1]
    R = S // NCORES
    x2 = np.ascontiguousarray(x.reshape(S, D))
    wq_sel = np.ascontiguousarray(
        Wq.reshape(32, 128, 32, 128)[::4]
        .transpose(2, 3, 0, 1)).reshape(NT, 128, QD)
    wo_spl = np.ascontiguousarray(Wo.T.reshape(H, 4, 128, D))
    wf_spl = np.ascontiguousarray(
        Wf.reshape(32, 128, 32, 128).transpose(0, 3, 2, 1)).reshape(NT, 128, D)
    bq_sel = np.ascontiguousarray(bq.reshape(32, 128)[::4].reshape(QD))

    nc = build_nc(R)
    in_maps = []
    for c in range(NCORES):
        xs = x2[c * R:(c + 1) * R]
        in_maps.append({
            "x": np.ascontiguousarray(xs),
            "xt": np.ascontiguousarray(
                xs.reshape(R // 128, 128, D // 128, 128).transpose(0, 3, 2, 1)),
            "wq": wq_sel, "bq": bq_sel, "wo": wo_spl, "m": M, "z": z,
            "wf": wf_spl, "sc": sc,
        })
    res = run_bass_kernel_spmd(nc, in_maps, core_ids=list(range(NCORES)))
    out = np.concatenate([np.asarray(r["y"]) for r in res.results], axis=0)
    return out.reshape(1, S, D).astype(np.float32)
